# version 15
# baseline (speedup 1.0000x reference)
import sys
import contextlib
import numpy as np
import ml_dtypes

try:
    import concourse.bass as bass
except ImportError:
    sys.path.insert(0, "/opt/trn_rl_repo")
    import concourse.bass as bass
import concourse.mybir as mybir
from concourse.bass_utils import run_bass_kernel_spmd
from concourse.alu_op_type import AluOpType as OP

F32 = mybir.dt.float32
BF16 = mybir.dt.bfloat16
I32 = mybir.dt.int32
AF = mybir.ActivationFunctionType

D = 4096
TPC = 2048
NCH = 4
TC = TPC // NCH
NT = TC // 128
NSL = 16
KTS = 16
NQ = 8
KTQ = 2
EPS = 1e-5
RSQ_MAGIC = 0x5F3759DF

SEMS = ["sW0", "sW1", "sWs0", "sWs1", "sX0", "sX1", "sY0", "sY1", "sM", "sM1", "sC0", "sC1",
        "sB", "sO", "sA", "sE0", "sE1", "sBw",
        "vW", "vW2", "vMM", "vEv", "vSd", "vTr", "vTe", "vNm", "vSt", "vAs",
        "vDs", "vPs", "vMk", "vTp", "vBi", "vFi", "vXf"]


def build():
    nc = bass.Bass("TRN2", target_bir_lowering=False, debug=False)

    x_d = nc.dram_tensor("x", [TPC, D], F32, kind="ExternalInput")
    w1s_d = nc.dram_tensor("w1s", [NSL, 128, KTS * 512], F32, kind="ExternalInput")
    w2s_d = nc.dram_tensor("w2s", [NSL, 128, KTS * 512], F32, kind="ExternalInput")
    mask_d = nc.dram_tensor("mask", [D, D], I32, kind="ExternalInput")
    bias_d = nc.dram_tensor("bias", [1, D], F32, kind="ExternalInput")
    idn_d = nc.dram_tensor("idn", [128, 128], BF16, kind="ExternalInput")
    p4_d = nc.dram_tensor("p4", [128, 4], BF16, kind="ExternalInput")
    ones1_d = nc.dram_tensor("ones1", [1, 128], F32, kind="ExternalInput")
    out_d = nc.dram_tensor("out", [TPC, D], F32, kind="ExternalOutput")
    cm_d = [nc.dram_tensor("c1m", [NSL, 128, KTS * 512], BF16, kind="Internal"),
            nc.dram_tensor("c2m", [NSL, 128, KTS * 512], BF16, kind="Internal")]
    btd_d = nc.dram_tensor("btd", [128, 16], BF16, kind="Internal")
    btea_d = nc.dram_tensor("btea", [8, 128, 512], BF16, kind="Internal")

    es = contextlib.ExitStack()
    with es:
        def sb(name, shape, dt):
            return es.enter_context(nc.sbuf_tensor(name, shape, dt))
        HT = sb("HT", [128, 32 * 512], BF16)
        WSL = [sb(f"WSL{i}", [128, KTS * 512], BF16) for i in range(2)]
        WST = [sb(f"WST{i}", [128, KTQ * 512], F32) for i in range(2)]
        ARAW = sb("ARAW", [128, NT * D], BF16)
        XP = sb("XP", [128, NT * D], BF16)
        XS = [sb(f"XS{i}", [128, 2048], F32) for i in range(2)]
        HS = sb("HS", [128, D], BF16)
        OS = sb("OS", [128, D], F32)
        BIASF = sb("BIASF", [128, D], BF16)
        MHI = sb("MHI", [128, 3584], I32)
        COLRED = sb("COLRED", [128, 512], BF16)
        BRT = sb("BRT", [4, 512], BF16)
        BT = sb("BT", [128, 128], BF16)
        BTE = [sb(f"BTE{i}", [128, 512], BF16) for i in range(2)]
        IDN = sb("IDN", [128, 128], BF16)
        P4T = sb("P4T", [128, 4], BF16)
        ONES1 = sb("ONES1", [1, 128], F32)
        BNST = sb("BNST", [128, 48 * 16], F32)
        MV = sb("MV", [128, 32], F32)
        RS = sb("RS", [128, 16], F32)
        AUX = sb("AUX", [128, 16], F32)
        NEWT = sb("NEWT", [128, 64], F32)
        XSQ = sb("XSQ", [128, 4], F32)

        pmm = es.enter_context(nc.psum_tensor("pmm", [128, 4 * 512], F32))
        ptA = es.enter_context(nc.psum_tensor("ptA", [128, 1024], F32))
        ptB = es.enter_context(nc.psum_tensor("ptB", [128, 1024], F32))

        sem = {n: es.enter_context(nc.semaphore(n)) for n in SEMS}
        cnt = {n: 0 for n in SEMS}

        def inc(ins, n, k=1):
            cnt[n] += k
            ins.then_inc(sem[n], k)
            return (n, cnt[n])

        def wt(eng, marker):
            if marker is None:
                return
            n, v = marker
            if v > 0:
                eng.wait_ge(sem[n], v)

        blk = es.enter_context(nc.Block())

        OSi32 = OS[:, :].bitcast(I32)

        state = {"slab_uses": 0, "wst_uses": 0, "xs_uses": 0,
                 "tr_groups": 0, "h_tiles": 0, "og": 0}
        slab_done = {}
        cachew_use = {}
        cachew_data = {}
        prev_ev = [None]
        bte_freed = {}
        bte_freed_p = {}
        bte_ready = {}
        btea_w = {}
        wst_sched = []

        def vds_edge(v, ins):
            m = inc(ins, "vDs", 1)
            v.wait_ge(sem["vDs"], m[1])

        @blk.gpsimd
        def _(g):
            g.dma_start(IDN[:, :], idn_d[:, :]).then_inc(sem["sA"], 16)
            g.dma_start(P4T[:, :], p4_d[:, :]).then_inc(sem["sA"], 16)
            g.dma_start(ONES1[:, :], ones1_d[:, :]).then_inc(sem["sA"], 16)
            g.dma_start(OS[0:1, :], bias_d[:, :]).then_inc(sem["sA"], 16)
        cnt["sA"] = 64

        vds0 = cnt["vDs"]
        bi_marks = []

        @blk.tensor
        def _(t):
            t.wait_ge(sem["sA"], 64)
            for g4 in range(4):
                if g4 > 0:
                    t.wait_ge(sem["vDs"], vds0 + g4)
                for h in range(2):
                    nlo = g4 * 1024 + h * 512
                    ins = t.matmul(ptB[:, h * 512:(h + 1) * 512], ONES1[:, :],
                                   OS[0:1, nlo:nlo + 512], start=True, stop=True)
                bi_marks.append(inc(ins, "vBi", 1))

        @blk.vector
        def _(v):
            for g4 in range(4):
                wt(v, bi_marks[g4])
                ins = v.tensor_copy(BIASF[:, g4 * 1024:(g4 + 1) * 1024], ptB[:, :])
                inc(ins, "vDs", 1)
        bias_ptB_free = ("vDs", cnt["vDs"])
        bias_os_free = bi_marks[-1]


        def emit_rsqrt_block(v, rs_ap, var_ap, vpe, y2, iters):
            y = rs_ap
            ins = v.tensor_scalar(vpe, var_ap, 1.0, EPS, OP.mult, OP.add)
            vds_edge(v, ins)
            ins = v.tensor_scalar(y.bitcast(I32), vpe.bitcast(I32),
                                  1, None, OP.arith_shift_right)
            vds_edge(v, ins)
            ins = v.tensor_scalar(y.bitcast(I32), y.bitcast(I32),
                                  -1, RSQ_MAGIC, OP.mult, OP.add)
            vds_edge(v, ins)
            for _it in range(iters):
                ins = v.tensor_tensor(y2, y, y, op=OP.mult)
                vds_edge(v, ins)
                ins = v.tensor_tensor(y2, y2, vpe, op=OP.mult)
                vds_edge(v, ins)
                ins = v.tensor_scalar(y2, y2, -0.5, 1.5, OP.mult, OP.add)
                vds_edge(v, ins)
                ins = v.tensor_tensor(y, y, y2, op=OP.mult)
                vds_edge(v, ins)
            return ins

        def emit_site_chain(v, site, iters):
            var4 = bass.AP(MV[:, :].tensor, site * 8 + 1, [[32, 128], [2, 4]])
            nb = site * 16
            return emit_rsqrt_block(v, RS[:, site * 4:site * 4 + 4], var4,
                                    NEWT[:, nb:nb + 4], NEWT[:, nb + 4:nb + 8],
                                    iters)

        def emit_aux4(v, site, sign):
            m4 = bass.AP(MV[:, :].tensor, site * 8, [[32, 128], [2, 4]])
            a4 = AUX[:, site * 4:site * 4 + 4]
            ins = v.tensor_tensor(a4, m4, RS[:, site * 4:site * 4 + 4], op=OP.mult)
            if sign < 0:
                vds_edge(v, ins)
                ins = v.tensor_scalar(a4, a4, -1.0, None, OP.mult)
            return ins

        def emit_h_transposes_grp(t, tt, grp, norm_mark, single_pt, extra):
            G = state["tr_groups"]
            state["tr_groups"] += 1
            wt(t, norm_mark)
            for m in extra:
                wt(t, m)
            if single_pt:
                if G >= 1:
                    t.wait_ge(sem["vTe"], G)
                pt = ptA[:, :].bitcast(BF16)
            else:
                if G >= 2:
                    t.wait_ge(sem["vTe"], G - 1)
                pt = (ptA if grp == 0 else ptB)[:, :].bitcast(BF16)
            for j in range(16):
                ktg = grp * 16 + j
                ins = t.transpose(pt[:, j * 128:(j + 1) * 128],
                                  HS[:, ktg * 128:(ktg + 1) * 128], IDN[:, :])
            return inc(ins, "vTr", 1)

        def emit_h_evac_grp(eng, tt, grp, tr_mark, single_pt):
            wt(eng, tr_mark)
            pt = (ptA if (single_pt or grp == 0) else ptB)[:, :].bitcast(BF16)
            src = pt[:, 0:2048].rearrange("p (k r) -> p k r", r=128)
            base = grp * 16 * 512 + tt * 128
            dst = bass.AP(HT[:, :].tensor, base, [[16384, 128], [512, 16], [1, 128]])
            ins = eng.activation(dst, src, AF.Identity)
            return inc(ins, "vTe", 1)

        def emit_mask_chain(nsl, first):
            halv = {}

            @blk.gpsimd
            def _(g):
                stg = [OSi32, XP[:, 0:8192].bitcast(I32)]
                ld = {}

                def issue(r):
                    rt = nsl * 4 + r
                    if rt == 0:
                        wt(g, bias_os_free)
                    sn = "sM" if rt % 2 == 0 else "sM1"
                    g.dma_start(stg[rt % 2][:, 0:4096],
                                mask_d[rt * 128:(rt + 1) * 128, :]
                                ).then_inc(sem[sn], 16)
                    cnt[sn] += 16
                    ld[r] = (sn, cnt[sn])

                issue(0)
                for r in range(4):
                    rt = nsl * 4 + r
                    if r + 1 < 4:
                        issue(r + 1)
                    wt(g, ld[r])
                    if r == 0 and nsl > 0:
                        g.wait_ge(sem["vAs"], nsl)
                    spans = [(0, 2048), (2048, 1024), (0, 512), (2048, 256),
                             (3072 + r * 128, 128)]
                    cur = stg[rt % 2][:, 0:4096]
                    for hstep in range(5):
                        dst_off, dst_n = spans[hstep]
                        a = cur.rearrange("p (a two) -> p a two", two=2)
                        ins = g.tensor_tensor(MHI[:, dst_off:dst_off + dst_n],
                                              a[:, :, 0], a[:, :, 1], op=OP.add)
                        m = inc(ins, "vPs", 1)
                        g.wait_ge(sem["vPs"], m[1])
                        cur = MHI[:, dst_off:dst_off + dst_n]
                halv[nsl] = ("vPs", cnt["vPs"])

            conv = {}

            @blk.scalar
            def _(s):
                wt(s, halv[nsl])
                if nsl >= 1:
                    s.wait_ge(sem["vTp"], 5 * (nsl - 1) + 1)
                ins = s.activation(COLRED[:, :], MHI[:, 3072:3584], AF.Identity)
                conv[nsl] = inc(ins, "vAs", 1)

            @blk.tensor
            def _(t):
                wt(t, conv[nsl])
                if first:
                    wt(t, bias_ptB_free)
                if nsl >= 1:
                    t.wait_ge(sem["vMk"], 2 * (nsl - 1) + 1)
                ins = t.matmul(ptB[0:4, 0:512], P4T[:, :], COLRED[:, :],
                               start=True, stop=True)
                inc(ins, "vTp", 1)
                t.wait_ge(sem["vMk"], 2 * nsl + 1)
                for r in range(4):
                    ins = t.transpose(ptB[:, :].bitcast(BF16)[:, 1024 + r * 4:
                                                              1028 + r * 4],
                                      BRT[0:4, r * 128:(r + 1) * 128],
                                      IDN[0:4, 0:4])
                    inc(ins, "vTp", 1)

            @blk.vector
            def _(v):
                v.wait_ge(sem["vTp"], 5 * nsl + 1)
                ins = v.tensor_scalar(BRT[0:4, :], ptB[0:4, 0:512], 0.0, None,
                                      OP.is_gt)
                inc(ins, "vMk", 1)
                v.wait_ge(sem["vTp"], 5 * nsl + 5)
                ins = v.tensor_copy(BT[:, 16 * nsl:16 * nsl + 16],
                                    ptB[:, :].bitcast(BF16)[:, 1024:1040])
                inc(ins, "vMk", 1)

            res = {}

            @blk.gpsimd
            def _(g):
                g.wait_ge(sem["vMk"], 2 * nsl + 2)
                g.wait_ge(sem["sB"], cnt["sB"])
                if nsl >= 2:
                    wt(g, bte_freed[(0, nsl - 2)])
                    wt(g, btea_w[nsl - 2])
                bteb = BTE[nsl % 2]
                g.dma_start(btd_d[:, :], BT[:, 16 * nsl:16 * nsl + 16]
                            ).then_inc(sem["sB"], 16)
                cnt["sB"] += 16
                g.wait_ge(sem["sB"], cnt["sB"])
                for a in range(4):
                    out_ap = bass.AP(bteb[:, :].tensor, 32 * a * 512,
                                     [[512, 32], [16, 32], [1, 16]])
                    in_ap = bass.AP(btd_d[:, :].tensor, a * 16,
                                    [[0, 32], [64, 32], [1, 16]])
                    g.dma_start(out_ap, in_ap).then_inc(sem["sB"], 16)
                    cnt["sB"] += 16
                res["m"] = ("sB", cnt["sB"])
                g.wait_ge(sem["sB"], cnt["sB"])
                g.wait_ge(sem["sBw"], cnt["sBw"])
                g.dma_start(btea_d[nsl, :, :], bteb[:, :]).then_inc(sem["sBw"], 16)
                cnt["sBw"] += 16
                btea_w[nsl] = ("sBw", cnt["sBw"])

            bte_ready[(0, nsl)] = res["m"]

        def emit_w_load_prep(c, layer, sl, slab_free=None):
            su = state["slab_uses"]
            state["slab_uses"] += 1
            buf = WSL[su % 2]
            nsl, kh = sl // 2, sl % 2
            free_marks = []
            if su >= 2:
                free_marks.append(slab_free or slab_done.get(su - 2))
                free_marks.append(cachew_use.get(su - 2))

            if layer == 1:
                wr = 0 if sl < 3 else (1 if sl < 9 else 2)
            else:
                wr = 1 if sl < 6 else 2

            if c <= wr:
                q_load = {}
                eng_sem = "vW" if layer == 0 else "vW2"
                base = cnt[eng_sem]
                for q in range(NQ):
                    wst_sched.append((eng_sem, base + q + 1))

                if layer == 0 and c >= 1 and kh == 0:
                    @blk.sync
                    def _(s):
                        wt(s, bte_freed_p[nsl % 2])
                        s.wait_ge(sem["sBw"], 16 * 8)
                        sn = "sE%d" % (nsl % 2)
                        s.dma_start(BTE[nsl % 2][:, :], btea_d[nsl, :, :]
                                    ).then_inc(sem[sn], 16)
                        cnt[sn] += 16
                        bte_ready[(c, nsl)] = (sn, cnt[sn])

                @blk.sync
                def _(s):
                    for q in range(NQ):
                        u = state["wst_uses"]
                        state["wst_uses"] += 1
                        prior = u - 2
                        if prior >= 0:
                            ps, pv = wst_sched[prior]
                            s.wait_ge(sem[ps], pv)
                        sn = "sW%d" % (u % 2)
                        s.dma_start(WST[u % 2][:, :],
                                    (w1s_d if layer == 0 else w2s_d)[
                                        sl, :, q * 1024:(q + 1) * 1024]
                                    ).then_inc(sem[sn], 16)
                        cnt[sn] += 16
                        q_load[q] = (sn, cnt[sn])
                wfirst = state["wst_uses"] - NQ

                res = {}
                if layer == 0:
                    @blk.vector
                    def _(v):
                        for q in range(NQ):
                            wt(v, q_load[q])
                            if q == 0:
                                wt(v, bte_ready[(c, nsl)])
                                for fm in free_marks:
                                    wt(v, fm)
                            j0 = q * KTQ
                            dst = buf[:, j0 * 512:(j0 + KTQ) * 512].rearrange(
                                "p (j nb i) -> p j nb i", j=KTQ, i=32)
                            src = WST[(wfirst + q) % 2][:, :].rearrange(
                                "p (j nb i) -> p j nb i", j=KTQ, i=32)
                            mk = BTE[nsl % 2][:, (kh * 16 + j0) * 16:
                                              (kh * 16 + j0 + KTQ) * 16
                                              ].rearrange("p (j nb) -> p j nb",
                                                          j=KTQ
                                              ).unsqueeze(3).broadcast_to(
                                              [128, KTQ, 16, 32])
                            ins = v.tensor_tensor(dst, src, mk, op=OP.mult)
                            inc(ins, "vW", 1)
                        res["m"] = ("vW", cnt["vW"])
                        if kh == 1:
                            bte_freed[(c, nsl)] = ("vW", cnt["vW"])
                            bte_freed_p[nsl % 2] = ("vW", cnt["vW"])
                else:
                    @blk.scalar
                    def _(s):
                        for q in range(NQ):
                            wt(s, q_load[q])
                            if q == 0:
                                for fm in free_marks:
                                    wt(s, fm)
                            j0 = q * KTQ
                            ins = s.activation(buf[:, j0 * 512:(j0 + KTQ) * 512],
                                               WST[(wfirst + q) % 2][:, :], AF.Sign)
                            inc(ins, "vW2", 1)
                        res["m"] = ("vW2", cnt["vW2"])

                ready = res["m"]

                if c == wr:
                    @blk.gpsimd
                    def _(g):
                        wt(g, ready)
                        sn = "sC%d" % (su % 2)
                        g.dma_start(cm_d[layer][sl, :, :], buf[:, :]
                                    ).then_inc(sem[sn], 16)
                        cnt[sn] += 16
                        cachew_use[su] = (sn, cnt[sn])
                        cachew_data[(layer, sl)] = (sn, cnt[sn])
            else:
                res = {}

                @blk.sync
                def _(s):
                    for fm in free_marks:
                        wt(s, fm)
                    if c == wr + 1:
                        wt(s, cachew_data[(layer, sl)])
                    sn = "sWs%d" % (su % 2)
                    s.dma_start(buf[:, :], cm_d[layer][sl, :, :]
                                ).then_inc(sem[sn], 16)
                    cnt[sn] += 16
                    res["m"] = (sn, cnt[sn])
                ready = res["m"]

            return ready, su

        def acc_bank(use8, nsl, tt):
            return pmm[:, tt * 512:(tt + 1) * 512]

        def emit_slab_mm(c, layer, sl, ready, su, ev_at, ht_mark, site,
                         pre_tt=None, use8=False, tr_base=0, bank_guard=None):
            nsl, kh = sl // 2, sl % 2
            mm_marks = {}

            interleave = pre_tt is not None and c >= 1

            @blk.tensor
            def _(t):
                if pre_tt is not None and not interleave:
                    for tt in range(NT):
                        pre_tt(t, tt)
                if not interleave:
                    wt(t, ready)
                for tt in range(NT):
                    if interleave:
                        pre_tt(t, tt)
                        if tt == 0:
                            wt(t, ready)
                    if kh == 0:
                        wt(t, ht_mark(tt))
                        prev_use = nsl - 1
                        if prev_use >= 0:
                            wt(t, ev_at.get((prev_use, tt)))
                        elif nsl == 0 and prev_ev[0] is not None:
                            wt(t, prev_ev[0][tt])
                    for j in range(KTS):
                        st_ap = HT[:, (kh * KTS + j) * 512 + tt * 128:
                                   (kh * KTS + j) * 512 + tt * 128 + 128]
                        mv_ap = WSL[su % 2][:, j * 512:(j + 1) * 512]
                        ins = t.matmul(acc_bank(use8, nsl, tt),
                                       st_ap, mv_ap,
                                       start=(kh == 0 and j == 0),
                                       stop=(kh == 1 and j == KTS - 1))
                    if kh == 1 and tt < NT - 1:
                        mm_marks[tt] = inc(ins, "vMM", 1)
                if kh == 1:
                    mm_marks[NT - 1] = inc(ins, "vMM", 1)
                    slab_done[su] = ("vMM", cnt["vMM"])
                else:
                    slab_done[su] = inc(ins, "vSd", 1)

            if kh == 1:
                @blk.vector
                def _(v):
                    wt(v, ("vFi", state["og"]))
                    ins = None
                    for tt in range(NT):
                        wt(v, mm_marks[tt])
                        dst = ARAW[:, tt * D + nsl * 512: tt * D + (nsl + 1) * 512]
                        if layer == 0:
                            ins = v.tensor_copy(dst, acc_bank(use8, nsl, tt))
                        else:
                            ins = v.tensor_tensor(dst, acc_bank(use8, nsl, tt),
                                                  BIASF[:, nsl * 512:(nsl + 1) * 512],
                                                  op=OP.add)
                        ev_at[(nsl, tt)] = inc(ins, "vEv", 1)
                    wt(v, ev_at[(nsl, NT - 1)])
                    for tt in range(NT):
                        sslot = site * 4 + tt
                        ins = v.bn_stats(BNST[:, sslot * 48 + nsl * 6:
                                             sslot * 48 + (nsl + 1) * 6],
                                         ARAW[:, tt * D + nsl * 512:
                                              tt * D + (nsl + 1) * 512])
                    if nsl == 7:
                        ev_at["stats_done"] = inc(ins, "vDs", 1)

        p0_saved = {}
        p0_xload = {}

        def emit_p0_part1(c, tts, phase="both"):
            t0 = c * TC
            if c not in p0_saved:
                p0_saved[c] = ({}, {})
                p0_xload[c] = {}
            p0_first_map, p0_chain = p0_saved[c]
            x_load = p0_xload[c]

            def loads(eng, tt):
                for hf in range(2):
                    u = state["xs_uses"]
                    state["xs_uses"] += 1
                    if u >= 2:
                        eng.wait_ge(sem["vXf"], u - 1)
                    sn = ("sY%d" if c == 0 else "sX%d") % (u % 2)
                    eng.dma_start(XS[u % 2][:, :],
                                  x_d[t0 + tt * 128: t0 + (tt + 1) * 128,
                                      hf * 2048:(hf + 1) * 2048]
                                  ).then_inc(sem[sn], 16)
                    cnt[sn] += 16
                    x_load[(tt, hf)] = (sn, cnt[sn])
                    p0_first_map[tt * 2 + hf] = u

            if phase != "stats":
                if c == 0:
                    for tt in tts:
                        @blk.sync
                        def _(s, tt=tt):
                            loads(s, tt)
                else:
                    @blk.gpsimd
                    def _(g):
                        for tt in tts:
                            loads(g, tt)
            if phase == "loads":
                return

            @blk.vector
            def _(v):
                ins = None
                for tt in tts:
                    bn = BNST[:, tt * 48:(tt + 1) * 48]
                    for hf in range(2):
                        wt(v, x_load[(tt, hf)])
                        xb = XS[p0_first_map[tt * 2 + hf] % 2]
                        for q in range(4):
                            ins = v.bn_stats(bn[:, (hf * 4 + q) * 6:
                                                (hf * 4 + q + 1) * 6],
                                             xb[:, q * 512:(q + 1) * 512])
                    vds_edge(v, ins)
                    ins = v.bn_aggr(MV[:, tt * 2:tt * 2 + 2], bn[:, :])
                    vds_edge(v, ins)
                    emit_rsqrt_block(v, RS[:, tt:tt + 1],
                                     MV[:, tt * 2 + 1:tt * 2 + 2],
                                     NEWT[:, tt:tt + 1], NEWT[:, 8 + tt:9 + tt],
                                     iters=1)
                    m0 = bass.AP(MV[:, :].tensor, tt * 2, [[32, 128], [1, 1]])
                    ins = v.tensor_tensor(AUX[:, tt:tt + 1], m0,
                                          RS[:, tt:tt + 1], op=OP.mult)
                    vds_edge(v, ins)
                    ins = v.tensor_scalar(AUX[:, tt:tt + 1], AUX[:, tt:tt + 1],
                                          -1.0, None, OP.mult)
                    p0_chain[tt] = inc(ins, "vSt", 1)

        def emit_p0_part2(c):
            p0_first_map, p0_chain = p0_saved.pop(c)
            p0_norm = {}
            g_h0 = state["h_tiles"]
            tr_now = state["tr_groups"]
            sp = (c == 0)

            @blk.scalar
            def _(s):
                for tt in range(NT):
                    wt(s, p0_chain[tt])
                    g_h = g_h0 + tt
                    if g_h >= 1:
                        s.wait_ge(sem["vTr"], 2 * g_h)
                    xa = XS[p0_first_map[tt * 2] % 2]
                    xb = XS[p0_first_map[tt * 2 + 1] % 2]
                    ins = s.activation(HS[:, 0:2048], xa[:, :], AF.Identity,
                                       bias=AUX[:, tt:tt + 1],
                                       scale=RS[:, tt:tt + 1])
                    inc(ins, "vXf", 1)
                    ins = s.activation(HS[:, 2048:4096], xb[:, :], AF.Identity,
                                       bias=AUX[:, tt:tt + 1],
                                       scale=RS[:, tt:tt + 1])
                    p0_norm[tt] = inc(ins, "vXf", 1)
                    for grp in range(2):
                        emit_h_evac_grp(s, tt, grp,
                                        ("vTr", tr_now + 2 * tt + grp + 1), sp)
            state["h_tiles"] += NT
            return p0_norm

        def emit_a2(c, t0, ev_at):
            a2_load = {}
            a2_first = state["xs_uses"]

            @blk.vector
            def _(v):
                wt(v, ev_at[(7, NT - 1)])
                wt(v, ev_at.get("stats_done"))
                ins = None
                for tt in range(NT):
                    sslot = 4 + tt
                    ins = v.bn_aggr(MV[:, 8 + tt * 2:8 + tt * 2 + 2],
                                    BNST[:, sslot * 48:(sslot + 1) * 48])
                vds_edge(v, ins)
                emit_site_chain(v, 1, iters=2)
                ins = emit_aux4(v, 1, +1)
                inc(ins, "vSt", 1)
            chain1 = ("vSt", cnt["vSt"])

            xp_mark = {}
            sq_mark = {}
            chain2 = {}
            a2_norm = {}
            g_h0 = state["h_tiles"]
            tr_a2 = state["tr_groups"]

            for tt in range(NT):
                @blk.gpsimd
                def _(g, tt=tt):
                    for hf in range(2):
                        u = state["xs_uses"]
                        state["xs_uses"] += 1
                        g.wait_ge(sem["vXf"], u - 1)
                        sn = "sX%d" % (u % 2)
                        g.dma_start(XS[u % 2][:, :],
                                    x_d[t0 + tt * 128: t0 + (tt + 1) * 128,
                                        hf * 2048:(hf + 1) * 2048]
                                    ).then_inc(sem[sn], 16)
                        cnt[sn] += 16
                        a2_load[(tt, hf)] = (sn, cnt[sn])

                @blk.vector
                def _(v, tt=tt):
                    wt(v, chain1)
                    wt(v, ("vFi", state["og"]))
                    r1 = RS[:, 4 + tt:4 + tt + 1]
                    mrs1 = AUX[:, 4 + tt:4 + tt + 1]
                    for hf in range(2):
                        u = a2_first + tt * 2 + hf
                        xbuf = XS[u % 2]
                        wt(v, a2_load[(tt, hf)])
                        dst = XP[:, tt * D + hf * 2048: tt * D + (hf + 1) * 2048]
                        srcr = ARAW[:, tt * D + hf * 2048:
                                    tt * D + (hf + 1) * 2048]
                        ins = v.tensor_scalar(srcr, srcr, r1, None, OP.mult)
                        vds_edge(v, ins)
                        ins = v.tensor_scalar(dst, xbuf[:, :], mrs1, None,
                                              OP.subtract)
                        vds_edge(v, ins)
                        ins = v.tensor_tensor(dst, dst, srcr, op=OP.add)
                        xp_mark[tt] = inc(ins, "vXf", 1)

                @blk.scalar
                def _(s, tt=tt):
                    wt(s, xp_mark[tt])
                    s.wait_ge(sem["vTr"], 2 * (g_h0 + tt))
                    ins = s.activation(HS[:, :], XP[:, tt * D:(tt + 1) * D],
                                       AF.Square, accum_out=XSQ[:, tt:tt + 1])
                    sq_mark[tt] = inc(ins, "vAs", 1)

                @blk.vector
                def _(v, tt=tt):
                    wt(v, sq_mark[tt])
                    nb = 32 + tt * 4
                    m0 = bass.AP(MV[:, :].tensor, tt * 2, [[32, 128], [1, 1]])
                    msq = NEWT[:, nb + 2:nb + 3]
                    ins = v.tensor_tensor(msq, m0, m0, op=OP.mult)
                    vds_edge(v, ins)
                    var = NEWT[:, nb + 3:nb + 4]
                    ins = v.tensor_scalar(var, XSQ[:, tt:tt + 1], 1.0 / D, None,
                                          OP.mult)
                    vds_edge(v, ins)
                    ins = v.tensor_tensor(var, var, msq, op=OP.subtract)
                    vds_edge(v, ins)
                    emit_rsqrt_block(v, RS[:, 8 + tt:8 + tt + 1], var,
                                     NEWT[:, nb:nb + 1], NEWT[:, nb + 1:nb + 2],
                                     iters=1)
                    a2c = AUX[:, 8 + tt:8 + tt + 1]
                    ins = v.tensor_tensor(a2c, m0, RS[:, 8 + tt:8 + tt + 1],
                                          op=OP.mult)
                    vds_edge(v, ins)
                    ins = v.tensor_scalar(a2c, a2c, -1.0, None, OP.mult)
                    chain2[tt] = inc(ins, "vSt", 1)

                @blk.scalar
                def _(s, tt=tt):
                    wt(s, chain2[tt])
                    ins = s.activation(HS[:, :], XP[:, tt * D:(tt + 1) * D],
                                       AF.Identity,
                                       bias=AUX[:, 8 + tt:8 + tt + 1],
                                       scale=RS[:, 8 + tt:8 + tt + 1])
                    a2_norm[tt] = inc(ins, "vNm", 1)
                    for grp in range(2):
                        emit_h_evac_grp(s, tt, grp,
                                        ("vTr", tr_a2 + 2 * tt + grp + 1),
                                        False)

            state["h_tiles"] += NT
            return a2_norm

        def emit_finalize(c, ev_at):
            t0 = c * TC

            @blk.vector
            def _(v):
                wt(v, ev_at[(7, NT - 1)])
                wt(v, ev_at.get("stats_done"))
                ins = None
                for tt in range(NT):
                    sslot = 12 + tt
                    ins = v.bn_aggr(MV[:, 24 + tt * 2:24 + tt * 2 + 2],
                                    BNST[:, sslot * 48:(sslot + 1) * 48])
                vds_edge(v, ins)
                emit_site_chain(v, 3, iters=2)
                ins = emit_aux4(v, 3, +1)
                inc(ins, "vSt", 1)
            chain3 = ("vSt", cnt["vSt"])

            fin = {}
            for tt in range(NT):
                my_og = state["og"]
                state["og"] += 1

                @blk.vector
                def _(v, tt=tt, my_og=my_og):
                    wt(v, chain3)
                    if my_og > 0:
                        v.wait_ge(sem["sO"], 16 * my_og)
                    r3 = RS[:, 12 + tt:12 + tt + 1]
                    mrs3 = AUX[:, 12 + tt:12 + tt + 1]
                    srcr = ARAW[:, tt * D:(tt + 1) * D]
                    ins = v.tensor_scalar(srcr, srcr, r3, None, OP.mult)
                    vds_edge(v, ins)
                    ins = v.tensor_scalar(OS[:, :], XP[:, tt * D:(tt + 1) * D],
                                          mrs3, None, OP.subtract)
                    vds_edge(v, ins)
                    ins = v.tensor_tensor(OS[:, :], OS[:, :], srcr, op=OP.add)
                    fin[tt] = inc(ins, "vFi", 1)

                @blk.scalar
                def _(s, tt=tt):
                    wt(s, fin[tt])
                    s.dma_start(out_d[t0 + tt * 128: t0 + (tt + 1) * 128, :],
                                OS[:, :]).then_inc(sem["sO"], 16)
                    cnt["sO"] += 16

        fin_pending = [None]

        for c in range(NCH):
            t0 = c * TC
            pre_w = []
            if c == 0:
                emit_p0_part1(0, list(range(NT)), phase="loads")
                emit_mask_chain(0, True)
                emit_mask_chain(1, False)
                emit_p0_part1(0, list(range(NT)), phase="stats")
            else:
                vSd0, vMM0 = cnt["vSd"], cnt["vMM"]

                def fwd_done(sl_prev):
                    if sl_prev < 0:
                        return None
                    if sl_prev % 2 == 0:
                        return ("vSd", vSd0 + sl_prev // 2 + 1)
                    return ("vMM", vMM0 + 4 * (sl_prev // 2 + 1))

                pre_w.append(emit_w_load_prep(c, 0, 0))
                pre_w.append(emit_w_load_prep(c, 0, 1))
                emit_p0_part1(c, [1, 2, 3])
                for sl2 in (2, 3):
                    pre_w.append(emit_w_load_prep(c, 0, sl2,
                                                  slab_free=fwd_done(sl2 - 2)))
            p0_norm = emit_p0_part2(c)
            a2_norm = None

            for layer in range(2):
                ev_at = {}
                site = 1 if layer == 0 else 3
                norm_marks = p0_norm if layer == 0 else a2_norm
                single_pt = (c == 0 and layer == 0)
                te_done = {}
                tr_base = state["tr_groups"]

                def pre_tt(t, tt, _nm=norm_marks, _sp=single_pt, _c=c,
                           _layer=layer, _te=te_done):
                    extra = []
                    if _c == 0 and _layer == 1 and tt == 0:
                        extra.append(("vMk", 16))
                    for grp in range(2):
                        _te[(tt, grp)] = emit_h_transposes_grp(
                            t, tt, grp, _nm[tt] if grp == 0 else None,
                            _sp, extra if grp == 0 else [])

                def ht_mark(tt, _tb=tr_base):
                    return ("vTe", _tb + 2 * tt + 2)

                if layer == 0 and pre_w:
                    pend = pre_w[0]
                    stash = pre_w[1:]
                else:
                    pend = emit_w_load_prep(c, layer, 0)
                    stash = []
                for sl in range(NSL):
                    if sl + 1 < NSL:
                        nxt = (stash[sl] if sl < len(stash)
                               else emit_w_load_prep(c, layer, sl + 1))
                    ready, su = pend
                    emit_slab_mm(c, layer, sl, ready, su, ev_at, ht_mark, site,
                                 pre_tt=pre_tt if sl == 0 else None)
                    if sl == 0:
                        if layer == 0 and fin_pending[0] is not None:
                            emit_finalize(*fin_pending[0])
                            fin_pending[0] = None
                    if c == 0 and layer == 0 and sl % 2 == 0 and sl // 2 + 2 <= 7:
                        emit_mask_chain(sl // 2 + 2, False)
                    if layer == 1 and sl == 8 and c + 1 < NCH:
                        emit_p0_part1(c + 1, [0])
                    if sl + 1 < NSL:
                        pend = nxt

                prev_ev[0] = {tt: ev_at[(7, tt)] for tt in range(NT)}

                if layer == 0:
                    a2_norm = emit_a2(c, t0, ev_at)
                else:
                    fin_pending[0] = (c, ev_at)

        emit_finalize(*fin_pending[0])

        @blk.scalar
        def _(s):
            s.wait_ge(sem["sO"], cnt["sO"])

    return nc


def kernel(L0_attn_data, L0_attn_weight, L0_attn_mask, L0_ffn_weight, L0_ffn_bias):
    x = np.ascontiguousarray(np.asarray(L0_attn_data, dtype=np.float32)
                             ).reshape(8, TPC, D)
    w1 = np.asarray(L0_attn_weight, dtype=np.float32)
    w2 = np.asarray(L0_ffn_weight, dtype=np.float32)
    mask = np.ascontiguousarray(np.asarray(L0_attn_mask, dtype=np.int32))
    bias = np.ascontiguousarray(np.asarray(L0_ffn_bias, dtype=np.float32
                                           ).reshape(1, D))

    def slabs(w):
        wt = np.ascontiguousarray(w.T).reshape(32, 128, D)
        out = np.empty((NSL, 128, KTS * 512), dtype=np.float32)
        for sl in range(NSL):
            nsl, kh = sl // 2, sl % 2
            b = wt[kh * 16:(kh + 1) * 16, :, nsl * 512:(nsl + 1) * 512]
            out[sl] = b.transpose(1, 0, 2).reshape(128, KTS * 512)
        return np.ascontiguousarray(out)

    w1s = slabs(w1)
    w2s = slabs(w2)

    idn = np.eye(128, dtype=ml_dtypes.bfloat16)
    p4 = np.zeros((128, 4), dtype=ml_dtypes.bfloat16)
    for r in range(128):
        p4[r, r // 32] = 1
    ones1 = np.ones((1, 128), dtype=np.float32)

    nc = build()
    in_maps = [{"x": np.ascontiguousarray(x[ci]), "w1s": w1s, "w2s": w2s,
                "mask": mask, "bias": bias, "idn": idn, "p4": p4,
                "ones1": ones1} for ci in range(8)]
    res = run_bass_kernel_spmd(nc, in_maps, core_ids=list(range(8)))
    out = np.concatenate([np.asarray(r["out"], dtype=np.float32)
                          for r in res.results], axis=0)
    return out.reshape(2, 8192, D)


# version 16
# speedup vs baseline: 1.2549x; 1.2549x over previous
import sys
import contextlib
import numpy as np
import ml_dtypes

try:
    import concourse.bass as bass
except ImportError:
    sys.path.insert(0, "/opt/trn_rl_repo")
    import concourse.bass as bass
import concourse.mybir as mybir
from concourse.bass_utils import run_bass_kernel_spmd
from concourse.alu_op_type import AluOpType as OP

F32 = mybir.dt.float32
BF16 = mybir.dt.bfloat16
I32 = mybir.dt.int32
AF = mybir.ActivationFunctionType

D = 4096
TPC = 2048
NCH = 4
TC = TPC // NCH
NT = TC // 128
NSL = 16
KTS = 16
NQ = 8
KTQ = 2
EPS = 1e-5
RSQ_MAGIC = 0x5F3759DF

SEMS = ["sW0", "sW1", "sWs0", "sWs1", "sX0", "sX1", "sY0", "sY1", "sM", "sM1", "sC0", "sC1",
        "sB", "sO", "sA",
        "vW", "vW2", "vMM", "vEv", "vSd", "vTr", "vTe", "vNm", "vSt", "vAs",
        "vDs", "vPs", "vMk", "vTp", "vBi", "vFi", "vXf"]


def build():
    nc = bass.Bass("TRN2", target_bir_lowering=False, debug=False)

    x_d = nc.dram_tensor("x", [TPC, D], F32, kind="ExternalInput")
    w1s_d = nc.dram_tensor("w1s", [NSL, 128, KTS * 512], F32, kind="ExternalInput")
    w2s_d = nc.dram_tensor("w2s", [NSL, 128, KTS * 512], F32, kind="ExternalInput")
    mask_d = nc.dram_tensor("mask", [D, D], I32, kind="ExternalInput")
    bias_d = nc.dram_tensor("bias", [1, D], F32, kind="ExternalInput")
    idn_d = nc.dram_tensor("idn", [128, 128], BF16, kind="ExternalInput")
    p4_d = nc.dram_tensor("p4", [128, 4], BF16, kind="ExternalInput")
    ones1_d = nc.dram_tensor("ones1", [1, 128], F32, kind="ExternalInput")
    out_d = nc.dram_tensor("out", [TPC, D], F32, kind="ExternalOutput")
    cm_d = [nc.dram_tensor("c1m", [NSL, 128, KTS * 512], BF16, kind="Internal"),
            nc.dram_tensor("c2m", [NSL, 128, KTS * 512], BF16, kind="Internal")]
    btd_d = nc.dram_tensor("btd", [128, 16], BF16, kind="Internal")

    es = contextlib.ExitStack()
    with es:
        def sb(name, shape, dt):
            return es.enter_context(nc.sbuf_tensor(name, shape, dt))
        HT = sb("HT", [128, 32 * 512], BF16)
        WSL = [sb(f"WSL{i}", [128, KTS * 512], BF16) for i in range(2)]
        WST = [sb(f"WST{i}", [128, KTQ * 512], F32) for i in range(2)]
        ARAW = sb("ARAW", [128, NT * D], BF16)
        XP = sb("XP", [128, NT * D], BF16)
        XS = [sb(f"XS{i}", [128, 2048], F32) for i in range(2)]
        HS = sb("HS", [128, D], BF16)
        OS = sb("OS", [128, D], F32)
        BIASF = sb("BIASF", [128, D], BF16)
        MHI = sb("MHI", [128, 3584], I32)
        COLRED = sb("COLRED", [128, 512], BF16)
        BRT = sb("BRT", [4, 512], BF16)
        BT = sb("BT", [128, 128], BF16)
        BTE = [sb(f"BTE{i}", [128, 512], BF16) for i in range(2)]
        IDN = sb("IDN", [128, 128], BF16)
        P4T = sb("P4T", [128, 4], BF16)
        ONES1 = sb("ONES1", [1, 128], F32)
        BNST = sb("BNST", [128, 48 * 16], F32)
        MV = sb("MV", [128, 32], F32)
        RS = sb("RS", [128, 16], F32)
        AUX = sb("AUX", [128, 16], F32)
        NEWT = sb("NEWT", [128, 64], F32)
        XSQ = sb("XSQ", [128, 4], F32)

        pmm = es.enter_context(nc.psum_tensor("pmm", [128, 4 * 512], F32))
        ptA = es.enter_context(nc.psum_tensor("ptA", [128, 1024], F32))
        ptB = es.enter_context(nc.psum_tensor("ptB", [128, 1024], F32))

        sem = {n: es.enter_context(nc.semaphore(n)) for n in SEMS}
        cnt = {n: 0 for n in SEMS}

        def inc(ins, n, k=1):
            cnt[n] += k
            ins.then_inc(sem[n], k)
            return (n, cnt[n])

        def wt(eng, marker):
            if marker is None:
                return
            n, v = marker
            if v > 0:
                eng.wait_ge(sem[n], v)

        blk = es.enter_context(nc.Block())

        OSi32 = OS[:, :].bitcast(I32)

        state = {"slab_uses": 0, "wst_uses": 0, "xs_uses": 0,
                 "tr_groups": 0, "h_tiles": 0, "og": 0}
        slab_done = {}
        cachew_use = {}
        cachew_data = {}
        prev_ev = [None]
        bte_freed = {}
        bte_ready = {}

        def vds_edge(v, ins):
            m = inc(ins, "vDs", 1)
            v.wait_ge(sem["vDs"], m[1])

        @blk.gpsimd
        def _(g):
            g.dma_start(IDN[:, :], idn_d[:, :]).then_inc(sem["sA"], 16)
            g.dma_start(P4T[:, :], p4_d[:, :]).then_inc(sem["sA"], 16)
            g.dma_start(ONES1[:, :], ones1_d[:, :]).then_inc(sem["sA"], 16)
            g.dma_start(OS[0:1, :], bias_d[:, :]).then_inc(sem["sA"], 16)
        cnt["sA"] = 64

        vds0 = cnt["vDs"]
        bi_marks = []

        @blk.tensor
        def _(t):
            t.wait_ge(sem["sA"], 64)
            for g4 in range(4):
                if g4 > 0:
                    t.wait_ge(sem["vDs"], vds0 + g4)
                for h in range(2):
                    nlo = g4 * 1024 + h * 512
                    ins = t.matmul(ptB[:, h * 512:(h + 1) * 512], ONES1[:, :],
                                   OS[0:1, nlo:nlo + 512], start=True, stop=True)
                bi_marks.append(inc(ins, "vBi", 1))

        @blk.vector
        def _(v):
            for g4 in range(4):
                wt(v, bi_marks[g4])
                ins = v.tensor_copy(BIASF[:, g4 * 1024:(g4 + 1) * 1024], ptB[:, :])
                inc(ins, "vDs", 1)
        bias_ptB_free = ("vDs", cnt["vDs"])
        bias_os_free = bi_marks[-1]


        def emit_rsqrt_block(v, rs_ap, var_ap, vpe, y2, iters):
            y = rs_ap
            ins = v.tensor_scalar(vpe, var_ap, 1.0, EPS, OP.mult, OP.add)
            vds_edge(v, ins)
            ins = v.tensor_scalar(y.bitcast(I32), vpe.bitcast(I32),
                                  1, None, OP.arith_shift_right)
            vds_edge(v, ins)
            ins = v.tensor_scalar(y.bitcast(I32), y.bitcast(I32),
                                  -1, RSQ_MAGIC, OP.mult, OP.add)
            vds_edge(v, ins)
            for _it in range(iters):
                ins = v.tensor_tensor(y2, y, y, op=OP.mult)
                vds_edge(v, ins)
                ins = v.tensor_tensor(y2, y2, vpe, op=OP.mult)
                vds_edge(v, ins)
                ins = v.tensor_scalar(y2, y2, -0.5, 1.5, OP.mult, OP.add)
                vds_edge(v, ins)
                ins = v.tensor_tensor(y, y, y2, op=OP.mult)
                vds_edge(v, ins)
            return ins

        def emit_site_chain(v, site, iters):
            var4 = bass.AP(MV[:, :].tensor, site * 8 + 1, [[32, 128], [2, 4]])
            nb = site * 16
            return emit_rsqrt_block(v, RS[:, site * 4:site * 4 + 4], var4,
                                    NEWT[:, nb:nb + 4], NEWT[:, nb + 4:nb + 8],
                                    iters)

        def emit_aux4(v, site, sign):
            m4 = bass.AP(MV[:, :].tensor, site * 8, [[32, 128], [2, 4]])
            a4 = AUX[:, site * 4:site * 4 + 4]
            ins = v.tensor_tensor(a4, m4, RS[:, site * 4:site * 4 + 4], op=OP.mult)
            if sign < 0:
                vds_edge(v, ins)
                ins = v.tensor_scalar(a4, a4, -1.0, None, OP.mult)
            return ins

        def emit_h_transposes_grp(t, tt, grp, norm_mark, single_pt, extra):
            G = state["tr_groups"]
            state["tr_groups"] += 1
            wt(t, norm_mark)
            for m in extra:
                wt(t, m)
            if single_pt:
                if G >= 1:
                    t.wait_ge(sem["vTe"], G)
                pt = ptA[:, :].bitcast(BF16)
            else:
                if G >= 2:
                    t.wait_ge(sem["vTe"], G - 1)
                pt = (ptA if grp == 0 else ptB)[:, :].bitcast(BF16)
            for j in range(16):
                ktg = grp * 16 + j
                ins = t.transpose(pt[:, j * 128:(j + 1) * 128],
                                  HS[:, ktg * 128:(ktg + 1) * 128], IDN[:, :])
            return inc(ins, "vTr", 1)

        def emit_h_evac_grp(eng, tt, grp, tr_mark, single_pt):
            wt(eng, tr_mark)
            pt = (ptA if (single_pt or grp == 0) else ptB)[:, :].bitcast(BF16)
            src = pt[:, 0:2048].rearrange("p (k r) -> p k r", r=128)
            base = grp * 16 * 512 + tt * 128
            dst = bass.AP(HT[:, :].tensor, base, [[16384, 128], [512, 16], [1, 128]])
            ins = eng.activation(dst, src, AF.Identity)
            return inc(ins, "vTe", 1)

        def emit_mask_chain(nsl, first):
            halv = {}

            @blk.gpsimd
            def _(g):
                stg = [OSi32, XP[:, 0:8192].bitcast(I32)]
                ld = {}

                def issue(r):
                    rt = nsl * 4 + r
                    if rt == 0:
                        wt(g, bias_os_free)
                    sn = "sM" if rt % 2 == 0 else "sM1"
                    g.dma_start(stg[rt % 2][:, 0:4096],
                                mask_d[rt * 128:(rt + 1) * 128, :]
                                ).then_inc(sem[sn], 16)
                    cnt[sn] += 16
                    ld[r] = (sn, cnt[sn])

                issue(0)
                for r in range(4):
                    rt = nsl * 4 + r
                    if r + 1 < 4:
                        issue(r + 1)
                    wt(g, ld[r])
                    if r == 0 and nsl > 0:
                        g.wait_ge(sem["vAs"], nsl)
                    spans = [(0, 2048), (2048, 1024), (0, 512), (2048, 256),
                             (3072 + r * 128, 128)]
                    cur = stg[rt % 2][:, 0:4096]
                    for hstep in range(5):
                        dst_off, dst_n = spans[hstep]
                        a = cur.rearrange("p (a two) -> p a two", two=2)
                        ins = g.tensor_tensor(MHI[:, dst_off:dst_off + dst_n],
                                              a[:, :, 0], a[:, :, 1], op=OP.add)
                        m = inc(ins, "vPs", 1)
                        g.wait_ge(sem["vPs"], m[1])
                        cur = MHI[:, dst_off:dst_off + dst_n]
                halv[nsl] = ("vPs", cnt["vPs"])

            conv = {}

            @blk.scalar
            def _(s):
                wt(s, halv[nsl])
                if nsl >= 1:
                    s.wait_ge(sem["vTp"], 5 * (nsl - 1) + 1)
                ins = s.activation(COLRED[:, :], MHI[:, 3072:3584], AF.Identity)
                conv[nsl] = inc(ins, "vAs", 1)

            @blk.tensor
            def _(t):
                wt(t, conv[nsl])
                if first:
                    wt(t, bias_ptB_free)
                if nsl >= 1:
                    t.wait_ge(sem["vMk"], 2 * (nsl - 1) + 1)
                ins = t.matmul(ptB[0:4, 0:512], P4T[:, :], COLRED[:, :],
                               start=True, stop=True)
                inc(ins, "vTp", 1)
                t.wait_ge(sem["vMk"], 2 * nsl + 1)
                for r in range(4):
                    ins = t.transpose(ptB[:, :].bitcast(BF16)[:, 1024 + r * 4:
                                                              1028 + r * 4],
                                      BRT[0:4, r * 128:(r + 1) * 128],
                                      IDN[0:4, 0:4])
                    inc(ins, "vTp", 1)

            @blk.vector
            def _(v):
                v.wait_ge(sem["vTp"], 5 * nsl + 1)
                ins = v.tensor_scalar(BRT[0:4, :], ptB[0:4, 0:512], 0.0, None,
                                      OP.is_gt)
                inc(ins, "vMk", 1)
                v.wait_ge(sem["vTp"], 5 * nsl + 5)
                ins = v.tensor_copy(BT[:, 16 * nsl:16 * nsl + 16],
                                    ptB[:, :].bitcast(BF16)[:, 1024:1040])
                inc(ins, "vMk", 1)

            res = {}

            @blk.gpsimd
            def _(g):
                g.wait_ge(sem["vMk"], 2 * nsl + 2)
                g.wait_ge(sem["sB"], cnt["sB"])
                if nsl >= 2:
                    wt(g, bte_freed[nsl - 2])
                bteb = BTE[nsl % 2]
                g.dma_start(btd_d[:, :], BT[:, 16 * nsl:16 * nsl + 16]
                            ).then_inc(sem["sB"], 16)
                cnt["sB"] += 16
                g.wait_ge(sem["sB"], cnt["sB"])
                for a in range(4):
                    out_ap = bass.AP(bteb[:, :].tensor, 32 * a * 512,
                                     [[512, 32], [16, 32], [1, 16]])
                    in_ap = bass.AP(btd_d[:, :].tensor, a * 16,
                                    [[0, 32], [64, 32], [1, 16]])
                    g.dma_start(out_ap, in_ap).then_inc(sem["sB"], 16)
                    cnt["sB"] += 16
                res["m"] = ("sB", cnt["sB"])

            bte_ready[nsl] = res["m"]

        def emit_w_load_prep(c, layer, sl):
            su = state["slab_uses"]
            state["slab_uses"] += 1
            buf = WSL[su % 2]
            nsl, kh = sl // 2, sl % 2
            free_marks = []
            if su >= 2:
                free_marks.append(slab_done.get(su - 2))
                free_marks.append(cachew_use.get(su - 2))

            if c == 0:
                q_load = {}
                L1B = NSL * NQ

                @blk.sync
                def _(s):
                    for q in range(NQ):
                        u = state["wst_uses"]
                        state["wst_uses"] += 1
                        prior = u - 2
                        if prior >= 0:
                            if prior >= L1B:
                                s.wait_ge(sem["vW2"], prior - L1B + 1)
                            else:
                                s.wait_ge(sem["vW"], prior + 1)
                        sn = "sW%d" % (u % 2)
                        s.dma_start(WST[u % 2][:, :],
                                    (w1s_d if layer == 0 else w2s_d)[
                                        sl, :, q * 1024:(q + 1) * 1024]
                                    ).then_inc(sem[sn], 16)
                        cnt[sn] += 16
                        q_load[q] = (sn, cnt[sn])
                wfirst = state["wst_uses"] - NQ

                res = {}
                if layer == 0:
                    @blk.vector
                    def _(v):
                        for q in range(NQ):
                            wt(v, q_load[q])
                            if q == 0:
                                wt(v, bte_ready[nsl])
                                for fm in free_marks:
                                    wt(v, fm)
                            j0 = q * KTQ
                            dst = buf[:, j0 * 512:(j0 + KTQ) * 512].rearrange(
                                "p (j nb i) -> p j nb i", j=KTQ, i=32)
                            src = WST[(wfirst + q) % 2][:, :].rearrange(
                                "p (j nb i) -> p j nb i", j=KTQ, i=32)
                            mk = BTE[nsl % 2][:, (kh * 16 + j0) * 16:
                                              (kh * 16 + j0 + KTQ) * 16
                                              ].rearrange("p (j nb) -> p j nb",
                                                          j=KTQ
                                              ).unsqueeze(3).broadcast_to(
                                              [128, KTQ, 16, 32])
                            ins = v.tensor_tensor(dst, src, mk, op=OP.mult)
                            inc(ins, "vW", 1)
                        res["m"] = ("vW", cnt["vW"])
                        if kh == 1:
                            bte_freed[nsl] = ("vW", cnt["vW"])
                else:
                    @blk.scalar
                    def _(s):
                        for q in range(NQ):
                            wt(s, q_load[q])
                            if q == 0:
                                for fm in free_marks:
                                    wt(s, fm)
                            j0 = q * KTQ
                            ins = s.activation(buf[:, j0 * 512:(j0 + KTQ) * 512],
                                               WST[(wfirst + q) % 2][:, :], AF.Sign)
                            inc(ins, "vW2", 1)
                        res["m"] = ("vW2", cnt["vW2"])

                ready = res["m"]

                @blk.gpsimd
                def _(g):
                    wt(g, ready)
                    sn = "sC%d" % (su % 2)
                    g.dma_start(cm_d[layer][sl, :, :], buf[:, :]
                                ).then_inc(sem[sn], 16)
                    cnt[sn] += 16
                    cachew_use[su] = (sn, cnt[sn])
                    cachew_data[(layer, sl)] = (sn, cnt[sn])
            else:
                res = {}

                @blk.sync
                def _(s):
                    for fm in free_marks:
                        wt(s, fm)
                    if c == 1:
                        wt(s, cachew_data[(layer, sl)])
                    sn = "sWs%d" % (su % 2)
                    s.dma_start(buf[:, :], cm_d[layer][sl, :, :]
                                ).then_inc(sem[sn], 16)
                    cnt[sn] += 16
                    res["m"] = (sn, cnt[sn])
                ready = res["m"]

            return ready, su

        def acc_bank(split, nsl, tt):
            if split and nsl % 2 == 1:
                return (ptA if tt < 2 else ptB)[:, (tt % 2) * 512:
                                                (tt % 2 + 1) * 512]
            return pmm[:, tt * 512:(tt + 1) * 512]

        def emit_slab_mm(c, layer, sl, ready, su, ev_at, ht_mark, site,
                         pre_tt=None, tr_base=0, bank_guard=None):
            nsl, kh = sl // 2, sl % 2
            mm_marks = {}
            split = not (c == 0 and layer == 0)

            interleave = pre_tt is not None and c >= 1

            @blk.tensor
            def _(t):
                if pre_tt is not None and not interleave:
                    for tt in range(NT):
                        pre_tt(t, tt)
                if not interleave:
                    wt(t, ready)
                for tt in range(NT):
                    if interleave:
                        pre_tt(t, tt)
                        if tt == 0:
                            wt(t, ready)
                    if kh == 0:
                        wt(t, ht_mark(tt))
                        if split and nsl % 2 == 1 and tt == 0:
                            wt(t, ht_mark(NT - 1))
                        prev_use = nsl - (2 if split else 1)
                        if prev_use >= 0:
                            wt(t, ev_at.get((prev_use, tt)))
                        elif nsl == 0 and prev_ev[0] is not None:
                            wt(t, prev_ev[0]["pm"][tt])
                    for j in range(KTS):
                        st_ap = HT[:, (kh * KTS + j) * 512 + tt * 128:
                                   (kh * KTS + j) * 512 + tt * 128 + 128]
                        mv_ap = WSL[su % 2][:, j * 512:(j + 1) * 512]
                        ins = t.matmul(acc_bank(split, nsl, tt),
                                       st_ap, mv_ap,
                                       start=(kh == 0 and j == 0),
                                       stop=(kh == 1 and j == KTS - 1))
                    if kh == 1 and tt < NT - 1:
                        mm_marks[tt] = inc(ins, "vMM", 1)
                if kh == 1:
                    mm_marks[NT - 1] = inc(ins, "vMM", 1)
                    slab_done[su] = ("vMM", cnt["vMM"])
                else:
                    slab_done[su] = inc(ins, "vSd", 1)

            if kh == 1:
                @blk.vector
                def _(v):
                    wt(v, ("vFi", state["og"]))
                    ins = None
                    for tt in range(NT):
                        wt(v, mm_marks[tt])
                        dst = ARAW[:, tt * D + nsl * 512: tt * D + (nsl + 1) * 512]
                        if layer == 0:
                            ins = v.tensor_copy(dst, acc_bank(split, nsl, tt))
                        else:
                            ins = v.tensor_tensor(dst, acc_bank(split, nsl, tt),
                                                  BIASF[:, nsl * 512:(nsl + 1) * 512],
                                                  op=OP.add)
                        ev_at[(nsl, tt)] = inc(ins, "vEv", 1)
                    wt(v, ev_at[(nsl, NT - 1)])
                    for tt in range(NT):
                        sslot = site * 4 + tt
                        ins = v.bn_stats(BNST[:, sslot * 48 + nsl * 6:
                                             sslot * 48 + (nsl + 1) * 6],
                                         ARAW[:, tt * D + nsl * 512:
                                              tt * D + (nsl + 1) * 512])
                    if nsl == 7:
                        ev_at["stats_done"] = inc(ins, "vDs", 1)

        p0_saved = {}
        p0_xload = {}

        def emit_p0_part1(c, tts, phase="both"):
            t0 = c * TC
            if c not in p0_saved:
                p0_saved[c] = ({}, {})
                p0_xload[c] = {}
            p0_first_map, p0_chain = p0_saved[c]
            x_load = p0_xload[c]

            def loads(eng, tt):
                for hf in range(2):
                    u = state["xs_uses"]
                    state["xs_uses"] += 1
                    if u >= 2:
                        eng.wait_ge(sem["vXf"], u - 1)
                    sn = ("sY%d" if c == 0 else "sX%d") % (u % 2)
                    eng.dma_start(XS[u % 2][:, :],
                                  x_d[t0 + tt * 128: t0 + (tt + 1) * 128,
                                      hf * 2048:(hf + 1) * 2048]
                                  ).then_inc(sem[sn], 16)
                    cnt[sn] += 16
                    x_load[(tt, hf)] = (sn, cnt[sn])
                    p0_first_map[tt * 2 + hf] = u

            if phase != "stats":
                if c == 0:
                    for tt in tts:
                        @blk.sync
                        def _(s, tt=tt):
                            loads(s, tt)
                else:
                    @blk.gpsimd
                    def _(g):
                        for tt in tts:
                            loads(g, tt)
            if phase == "loads":
                return

            @blk.vector
            def _(v):
                ins = None
                for tt in tts:
                    bn = BNST[:, tt * 48:(tt + 1) * 48]
                    for hf in range(2):
                        wt(v, x_load[(tt, hf)])
                        xb = XS[p0_first_map[tt * 2 + hf] % 2]
                        for q in range(4):
                            ins = v.bn_stats(bn[:, (hf * 4 + q) * 6:
                                                (hf * 4 + q + 1) * 6],
                                             xb[:, q * 512:(q + 1) * 512])
                    vds_edge(v, ins)
                    ins = v.bn_aggr(MV[:, tt * 2:tt * 2 + 2], bn[:, :])
                    vds_edge(v, ins)
                    emit_rsqrt_block(v, RS[:, tt:tt + 1],
                                     MV[:, tt * 2 + 1:tt * 2 + 2],
                                     NEWT[:, tt:tt + 1], NEWT[:, 8 + tt:9 + tt],
                                     iters=1)
                    m0 = bass.AP(MV[:, :].tensor, tt * 2, [[32, 128], [1, 1]])
                    ins = v.tensor_tensor(AUX[:, tt:tt + 1], m0,
                                          RS[:, tt:tt + 1], op=OP.mult)
                    vds_edge(v, ins)
                    ins = v.tensor_scalar(AUX[:, tt:tt + 1], AUX[:, tt:tt + 1],
                                          -1.0, None, OP.mult)
                    p0_chain[tt] = inc(ins, "vSt", 1)

        def emit_p0_part2(c):
            p0_first_map, p0_chain = p0_saved.pop(c)
            p0_norm = {}
            g_h0 = state["h_tiles"]
            tr_now = state["tr_groups"]
            sp = (c == 0)

            @blk.scalar
            def _(s):
                for tt in range(NT):
                    wt(s, p0_chain[tt])
                    g_h = g_h0 + tt
                    if g_h >= 1:
                        s.wait_ge(sem["vTr"], 2 * g_h)
                    xa = XS[p0_first_map[tt * 2] % 2]
                    xb = XS[p0_first_map[tt * 2 + 1] % 2]
                    ins = s.activation(HS[:, 0:2048], xa[:, :], AF.Identity,
                                       bias=AUX[:, tt:tt + 1],
                                       scale=RS[:, tt:tt + 1])
                    inc(ins, "vXf", 1)
                    ins = s.activation(HS[:, 2048:4096], xb[:, :], AF.Identity,
                                       bias=AUX[:, tt:tt + 1],
                                       scale=RS[:, tt:tt + 1])
                    p0_norm[tt] = inc(ins, "vXf", 1)
                    for grp in range(2):
                        emit_h_evac_grp(s, tt, grp,
                                        ("vTr", tr_now + 2 * tt + grp + 1), sp)
            state["h_tiles"] += NT
            return p0_norm

        def emit_a2(c, t0, ev_at):
            a2_load = {}
            a2_first = state["xs_uses"]

            @blk.vector
            def _(v):
                wt(v, ev_at[(7, NT - 1)])
                wt(v, ev_at.get("stats_done"))
                ins = None
                for tt in range(NT):
                    sslot = 4 + tt
                    ins = v.bn_aggr(MV[:, 8 + tt * 2:8 + tt * 2 + 2],
                                    BNST[:, sslot * 48:(sslot + 1) * 48])
                vds_edge(v, ins)
                emit_site_chain(v, 1, iters=2)
                ins = emit_aux4(v, 1, +1)
                inc(ins, "vSt", 1)
            chain1 = ("vSt", cnt["vSt"])

            xp_mark = {}
            sq_mark = {}
            chain2 = {}
            a2_norm = {}
            g_h0 = state["h_tiles"]
            tr_a2 = state["tr_groups"]

            for tt in range(NT):
                @blk.gpsimd
                def _(g, tt=tt):
                    for hf in range(2):
                        u = state["xs_uses"]
                        state["xs_uses"] += 1
                        g.wait_ge(sem["vXf"], u - 1)
                        sn = "sX%d" % (u % 2)
                        g.dma_start(XS[u % 2][:, :],
                                    x_d[t0 + tt * 128: t0 + (tt + 1) * 128,
                                        hf * 2048:(hf + 1) * 2048]
                                    ).then_inc(sem[sn], 16)
                        cnt[sn] += 16
                        a2_load[(tt, hf)] = (sn, cnt[sn])

                @blk.vector
                def _(v, tt=tt):
                    wt(v, chain1)
                    wt(v, ("vFi", state["og"]))
                    r1 = RS[:, 4 + tt:4 + tt + 1]
                    mrs1 = AUX[:, 4 + tt:4 + tt + 1]
                    for hf in range(2):
                        u = a2_first + tt * 2 + hf
                        xbuf = XS[u % 2]
                        wt(v, a2_load[(tt, hf)])
                        dst = XP[:, tt * D + hf * 2048: tt * D + (hf + 1) * 2048]
                        srcr = ARAW[:, tt * D + hf * 2048:
                                    tt * D + (hf + 1) * 2048]
                        ins = v.tensor_scalar(srcr, srcr, r1, None, OP.mult)
                        vds_edge(v, ins)
                        ins = v.tensor_scalar(dst, xbuf[:, :], mrs1, None,
                                              OP.subtract)
                        vds_edge(v, ins)
                        ins = v.tensor_tensor(dst, dst, srcr, op=OP.add)
                        xp_mark[tt] = inc(ins, "vXf", 1)

                @blk.scalar
                def _(s, tt=tt):
                    wt(s, xp_mark[tt])
                    s.wait_ge(sem["vTr"], 2 * (g_h0 + tt))
                    ins = s.activation(HS[:, :], XP[:, tt * D:(tt + 1) * D],
                                       AF.Square, accum_out=XSQ[:, tt:tt + 1])
                    sq_mark[tt] = inc(ins, "vAs", 1)

                @blk.vector
                def _(v, tt=tt):
                    wt(v, sq_mark[tt])
                    nb = 32 + tt * 4
                    m0 = bass.AP(MV[:, :].tensor, tt * 2, [[32, 128], [1, 1]])
                    msq = NEWT[:, nb + 2:nb + 3]
                    ins = v.tensor_tensor(msq, m0, m0, op=OP.mult)
                    vds_edge(v, ins)
                    var = NEWT[:, nb + 3:nb + 4]
                    ins = v.tensor_scalar(var, XSQ[:, tt:tt + 1], 1.0 / D, None,
                                          OP.mult)
                    vds_edge(v, ins)
                    ins = v.tensor_tensor(var, var, msq, op=OP.subtract)
                    vds_edge(v, ins)
                    emit_rsqrt_block(v, RS[:, 8 + tt:8 + tt + 1], var,
                                     NEWT[:, nb:nb + 1], NEWT[:, nb + 1:nb + 2],
                                     iters=1)
                    a2c = AUX[:, 8 + tt:8 + tt + 1]
                    ins = v.tensor_tensor(a2c, m0, RS[:, 8 + tt:8 + tt + 1],
                                          op=OP.mult)
                    vds_edge(v, ins)
                    ins = v.tensor_scalar(a2c, a2c, -1.0, None, OP.mult)
                    chain2[tt] = inc(ins, "vSt", 1)

                @blk.scalar
                def _(s, tt=tt):
                    wt(s, chain2[tt])
                    ins = s.activation(HS[:, :], XP[:, tt * D:(tt + 1) * D],
                                       AF.Identity,
                                       bias=AUX[:, 8 + tt:8 + tt + 1],
                                       scale=RS[:, 8 + tt:8 + tt + 1])
                    a2_norm[tt] = inc(ins, "vNm", 1)
                    for grp in range(2):
                        emit_h_evac_grp(s, tt, grp,
                                        ("vTr", tr_a2 + 2 * tt + grp + 1),
                                        False)

            state["h_tiles"] += NT
            return a2_norm

        def emit_finalize(c, ev_at):
            t0 = c * TC

            @blk.vector
            def _(v):
                wt(v, ev_at[(7, NT - 1)])
                wt(v, ev_at.get("stats_done"))
                ins = None
                for tt in range(NT):
                    sslot = 12 + tt
                    ins = v.bn_aggr(MV[:, 24 + tt * 2:24 + tt * 2 + 2],
                                    BNST[:, sslot * 48:(sslot + 1) * 48])
                vds_edge(v, ins)
                emit_site_chain(v, 3, iters=2)
                ins = emit_aux4(v, 3, +1)
                inc(ins, "vSt", 1)
            chain3 = ("vSt", cnt["vSt"])

            fin = {}
            for tt in range(NT):
                my_og = state["og"]
                state["og"] += 1

                @blk.vector
                def _(v, tt=tt, my_og=my_og):
                    wt(v, chain3)
                    if my_og > 0:
                        v.wait_ge(sem["sO"], 16 * my_og)
                    r3 = RS[:, 12 + tt:12 + tt + 1]
                    mrs3 = AUX[:, 12 + tt:12 + tt + 1]
                    srcr = ARAW[:, tt * D:(tt + 1) * D]
                    ins = v.tensor_scalar(srcr, srcr, r3, None, OP.mult)
                    vds_edge(v, ins)
                    ins = v.tensor_scalar(OS[:, :], XP[:, tt * D:(tt + 1) * D],
                                          mrs3, None, OP.subtract)
                    vds_edge(v, ins)
                    ins = v.tensor_tensor(OS[:, :], OS[:, :], srcr, op=OP.add)
                    fin[tt] = inc(ins, "vFi", 1)

                @blk.scalar
                def _(s, tt=tt):
                    wt(s, fin[tt])
                    s.dma_start(out_d[t0 + tt * 128: t0 + (tt + 1) * 128, :],
                                OS[:, :]).then_inc(sem["sO"], 16)
                    cnt["sO"] += 16

        fin_pending = [None]

        for c in range(NCH):
            t0 = c * TC
            if c == 0:
                emit_p0_part1(0, list(range(NT)), phase="loads")
                emit_mask_chain(0, True)
                emit_mask_chain(1, False)
                emit_p0_part1(0, list(range(NT)), phase="stats")
            else:
                emit_p0_part1(c, [1, 2, 3])
            p0_norm = emit_p0_part2(c)
            a2_norm = None

            for layer in range(2):
                ev_at = {}
                site = 1 if layer == 0 else 3
                norm_marks = p0_norm if layer == 0 else a2_norm
                single_pt = (c == 0 and layer == 0)
                te_done = {}
                tr_base = state["tr_groups"]

                def pre_tt(t, tt, _nm=norm_marks, _sp=single_pt, _c=c,
                           _layer=layer, _te=te_done):
                    extra = []
                    if _c == 0 and _layer == 1 and tt == 0:
                        extra.append(("vMk", 16))
                    if tt == 0 and prev_ev[0] is not None:
                        pt_m = prev_ev[0].get("pt")
                        if pt_m is not None:
                            extra.append(pt_m)
                    for grp in range(2):
                        _te[(tt, grp)] = emit_h_transposes_grp(
                            t, tt, grp, _nm[tt] if grp == 0 else None,
                            _sp, extra if grp == 0 else [])

                def ht_mark(tt, _tb=tr_base):
                    return ("vTe", _tb + 2 * tt + 2)

                pend = emit_w_load_prep(c, layer, 0)
                for sl in range(NSL):
                    if sl + 1 < NSL:
                        nxt = emit_w_load_prep(c, layer, sl + 1)
                    ready, su = pend
                    emit_slab_mm(c, layer, sl, ready, su, ev_at, ht_mark, site,
                                 pre_tt=pre_tt if sl == 0 else None)
                    if sl == 0:
                        if layer == 0 and fin_pending[0] is not None:
                            emit_finalize(*fin_pending[0])
                            fin_pending[0] = None
                    if c == 0 and layer == 0 and sl % 2 == 0 and sl // 2 + 2 <= 7:
                        emit_mask_chain(sl // 2 + 2, False)
                    if layer == 1 and sl == 8 and c + 1 < NCH:
                        emit_p0_part1(c + 1, [0])
                    if sl + 1 < NSL:
                        pend = nxt

                if single_pt:
                    prev_ev[0] = {"pm": {tt: ev_at[(7, tt)] for tt in range(NT)},
                                  "pt": None}
                else:
                    prev_ev[0] = {"pm": {tt: ev_at[(6, tt)] for tt in range(NT)},
                                  "pt": ev_at[(7, NT - 1)]}

                if layer == 0:
                    a2_norm = emit_a2(c, t0, ev_at)
                else:
                    fin_pending[0] = (c, ev_at)

        emit_finalize(*fin_pending[0])

        @blk.scalar
        def _(s):
            s.wait_ge(sem["sO"], cnt["sO"])

    return nc


def kernel(L0_attn_data, L0_attn_weight, L0_attn_mask, L0_ffn_weight, L0_ffn_bias):
    x = np.ascontiguousarray(np.asarray(L0_attn_data, dtype=np.float32)
                             ).reshape(8, TPC, D)
    w1 = np.asarray(L0_attn_weight, dtype=np.float32)
    w2 = np.asarray(L0_ffn_weight, dtype=np.float32)
    mask = np.ascontiguousarray(np.asarray(L0_attn_mask, dtype=np.int32))
    bias = np.ascontiguousarray(np.asarray(L0_ffn_bias, dtype=np.float32
                                           ).reshape(1, D))

    def slabs(w):
        wt = np.ascontiguousarray(w.T).reshape(32, 128, D)
        out = np.empty((NSL, 128, KTS * 512), dtype=np.float32)
        for sl in range(NSL):
            nsl, kh = sl // 2, sl % 2
            b = wt[kh * 16:(kh + 1) * 16, :, nsl * 512:(nsl + 1) * 512]
            out[sl] = b.transpose(1, 0, 2).reshape(128, KTS * 512)
        return np.ascontiguousarray(out)

    w1s = slabs(w1)
    w2s = slabs(w2)

    idn = np.eye(128, dtype=ml_dtypes.bfloat16)
    p4 = np.zeros((128, 4), dtype=ml_dtypes.bfloat16)
    for r in range(128):
        p4[r, r // 32] = 1
    ones1 = np.ones((1, 128), dtype=np.float32)

    nc = build()
    in_maps = [{"x": np.ascontiguousarray(x[ci]), "w1s": w1s, "w2s": w2s,
                "mask": mask, "bias": bias, "idn": idn, "p4": p4,
                "ones1": ones1} for ci in range(8)]
    res = run_bass_kernel_spmd(nc, in_maps, core_ids=list(range(8)))
    out = np.concatenate([np.asarray(r["out"], dtype=np.float32)
                          for r in res.results], axis=0)
    return out.reshape(2, 8192, D)
